# version 11
# baseline (speedup 1.0000x reference)
import contextlib
import numpy as np
import ml_dtypes
import concourse.bass as bass
import concourse.mybir as mybir
from concourse.bass_utils import run_bass_kernel_spmd
from concourse.bass import IndirectOffsetOnAxis

F32 = mybir.dt.float32
BF16 = mybir.dt.bfloat16
U32 = mybir.dt.uint32
AF = mybir.ActivationFunctionType
ALU = mybir.AluOpType
AX = mybir.AxisListType

NCORES = 8
D = 1024
KD = 8
SEQ = 2048
NSEQ = 4
HALF = 1024
TOK = 8192
NT = 64
E = 8
CAP = 16384
TRB = CAP + 128
H = 4096
KH = 32
CBL = 256
NCB = CAP // CBL
YCH = 2048
NCH = CAP // YCH
EPS = 1e-5
RG = [[0, 1, 2, 3, 4, 5, 6, 7]]

W_NAMES = ["wx_f", "wgx_f", "wga_f", "wo_f", "wx_b", "wgx_b", "wga_b", "wo_b", "ld_w"]
VPAR = ["bx_f", "bgx_f", "bga_f", "bo_f", "bx_b", "bgx_b", "bga_b", "bo_b",
        "ld_b", "alog_f", "alog_b"]


def build_program(debug=False, level=99, plain_gather=False, combine_mode=0):
    nc = bass.Bass("TRN2", target_bir_lowering=False)
    DT = nc.dram_tensor

    x_my = DT("x_my", [TOK, D], F32, kind="ExternalInput").ap()
    wmix = {n: DT(n, [D, D], F32, kind="ExternalInput").ap() for n in W_NAMES}
    vpar = {n: DT(n, [D], F32, kind="ExternalInput").ap() for n in VPAR}
    wg_in = DT("wg", [D, E], F32, kind="ExternalInput").ap()
    w1_my = DT("w1_my", [D, H], F32, kind="ExternalInput").ap()
    b1_my = DT("b1_my", [H], F32, kind="ExternalInput").ap()
    w2_my = DT("w2_my", [H, D], F32, kind="ExternalInput").ap()
    b2_my = DT("b2_my", [D], F32, kind="ExternalInput").ap()
    c_u128 = DT("c_u128", [128, 128], F32, kind="ExternalInput").ap()
    c_id = DT("c_id", [128, 128], F32, kind="ExternalInput").ap()
    c_onesrow = DT("c_onesrow", [1, 128], F32, kind="ExternalInput").ap()
    c_mask2 = DT("c_mask2", [8, 2], F32, kind="ExternalInput").ap()
    c_tokp = DT("c_tokp", [128, NT * 2], F32, kind="ExternalInput").ap()

    out_my = DT("out_my", [TOK, D], F32, kind="ExternalOutput").ap()

    xn_bf = DT("xn_bf", [TOK, D], BF16).ap()
    xn_all = DT("xn_all", [65537, D], BF16).ap()
    w1bf = DT("w1bf", [D, H], BF16).ap()
    w2bf = DT("w2bf", [H, D], BF16).ap()
    wbf = DT("wbf", [9, D, D], BF16).ap()
    t_full = DT("t_full", [E * TRB, 2], F32).ap()
    t_rs = DT("t_rs", [TRB, 2], F32).ap()
    tots_in = DT("tots_in", [1, 16], F32).ap()
    tots_out = DT("tots_out", [8, 16], F32).ap()
    d_my = DT("d_my", [CAP, D], BF16).ap()
    y_my = DT("y_my", [CAP, D], BF16).ap()
    y_all = DT("y_all", [E * CAP, D], BF16).ap()
    s_t = DT("s_t", [D, TOK], BF16).ap()

    dbg = {}
    if debug:
        dbg["xn"] = DT("dbg_xn", [TOK, D], BF16, kind="ExternalOutput").ap()
        dbg["trs"] = DT("dbg_trs", [TRB, 2], F32, kind="ExternalOutput").ap()
        dbg["dmy"] = DT("dbg_dmy", [CAP, D], BF16, kind="ExternalOutput").ap()
        dbg["ymy"] = DT("dbg_ymy", [CAP, D], BF16, kind="ExternalOutput").ap()
        dbg["st"] = DT("dbg_st", [D, TOK], BF16, kind="ExternalOutput").ap()
        dbg["lt"] = DT("dbg_lt", [128, NT * 8], F32, kind="ExternalOutput").ap()
        dbg["slot0"] = DT("dbg_slot0", [128, NT], U32, kind="ExternalOutput").ap()
        dbg["slot1"] = DT("dbg_slot1", [128, NT], U32, kind="ExternalOutput").ap()
        dbg["gate0"] = DT("dbg_gate0", [128, NT], F32, kind="ExternalOutput").ap()
        dbg["gate1"] = DT("dbg_gate1", [128, NT], F32, kind="ExternalOutput").ap()
        dbg["yidx0"] = DT("dbg_yidx0", [128, NT], U32, kind="ExternalOutput").ap()
        dbg["yidx1"] = DT("dbg_yidx1", [128, NT], U32, kind="ExternalOutput").ap()

    sb = nc.alloc_sbuf_tensor

    U128 = sb("U128", [128, 128], F32)
    ID = sb("IDN", [128, 128], F32)
    ONESROW = sb("ONESROW", [1, 128], F32)
    MASK2 = sb("MASK2", [8, 2], F32)
    TOKP = sb("TOKP", [128, NT * 2], F32)
    WGSB = sb("WGSB", [128, KD, E], BF16)
    BIAS = sb("BIAS", [128, len(VPAR), 8], F32)
    SA = sb("SA", [128, 2, 8], F32)
    C1 = sb("C1", [128, 1], F32)
    ZERO = sb("ZEROT", [128, 1032], F32)
    TSLOT0 = sb("TSLOT0", [128, NT], U32)
    TSLOT1 = sb("TSLOT1", [128, NT], U32)
    GATE0 = sb("GATE0", [128, NT], F32)
    GATE1 = sb("GATE1", [128, NT], F32)
    YIDX0 = sb("YIDX0", [128, NT], U32)
    YIDX1 = sb("YIDX1", [128, NT], U32)
    CARRY = sb("CARRY", [128, 8], F32)
    BSTG = sb("BSTG", [32, 11 * 128], F32)
    XIDX = sb("XIDX", [128, 2 * 128], U32)

    PS = [nc.alloc_psum_tensor(f"ps{i}", [128, 512], F32) for i in range(6)]
    VIDX = {n: i for i, n in enumerate(VPAR)}

    def bias_col(name):
        return BIAS[:, VIDX[name], :]

    stack = contextlib.ExitStack()
    SEMN = ["ldc", "ldg", "mst", "mm", "bka", "bkv", "eva", "evv", "dg", "ds", "cc", "sto", "pa",
            "pa0", "pa1", "xs0", "xs1", "xnf", "wl0", "wl1", "dt0", "dt1",
            "ys0", "ys1", "ga0", "ga1", "sc0", "sc1", "st0", "st1"]
    sem = {n: stack.enter_context(nc.semaphore(n)) for n in SEMN}
    cnt = {n: 0 for n in SEMN}
    block = stack.enter_context(nc.Block())

    def inc(instr, name, n=1):
        cnt[name] += n
        instr.then_inc(sem[name], n)
        return cnt[name]

    def snap():
        return dict(cnt)

    def join(eng, sn, only=None):
        for k, v in sn.items():
            if v > 0 and (only is None or k in only):
                eng.wait_ge(sem[k], v)

    @block.sync
    def _(sync):
        for aph, tile in [(c_u128, U128), (c_id, ID), (c_onesrow, ONESROW), (c_mask2, MASK2)]:
            inc(sync.dma_start(out=tile[:], in_=aph), "ldc", 16)
        inc(sync.dma_start(out=TOKP[:], in_=c_tokp), "ldc", 16)
        for i, n in enumerate(VPAR):
            inc(sync.dma_start(out=BSTG[0:8, i * 128:(i + 1) * 128],
                               in_=vpar[n].rearrange("(a p) -> a p", p=128)), "ldc", 16)
    ldc0 = cnt["ldc"]

    @block.vector
    def _(v):
        v.memset(C1[:], 1.0)
        v.memset(ZERO[:], 0.0)
        v.drain()
        inc(v.nop(), "evv", 1)
    evv_zero = cnt["evv"]

    @block.gpsimd
    def _(g):
        for i, n in enumerate(W_NAMES):
            inc(g.dma_start(out=wbf[i], in_=wmix[n]), "ldg", 16)
        inc(g.dma_start(out=WGSB[:], in_=wg_in.rearrange("(k p) e -> p k e", p=128)), "ldg", 16)
        inc(g.dma_start(out=w1bf, in_=w1_my), "ldg", 16)
        inc(g.dma_start(out=w2bf, in_=w2_my), "ldg", 16)
        g.wait_ge(sem["evv"], evv_zero)
        inc(g.dma_start(out=xn_all[65536:65537, :], in_=ZERO[0:1, 0:1024]), "ldg", 16)
        flat = t_full.rearrange("a b -> (a b)")
        for i in range(4):
            inc(g.dma_start(out=flat[i * 66048:(i + 1) * 66048].rearrange("(p f) -> p f", p=128),
                            in_=ZERO[:, 0:516]), "ldg", 16)
    ldg_all = cnt["ldg"]

    mm_b0 = cnt["mm"]

    @block.tensor
    def _(t):
        t.wait_ge(sem["ldc"], ldc0)
        for i in range(len(VPAR)):
            b = 4 + (i % 2)
            if i >= 2:
                t.wait_ge(sem["evv"], evv_zero + i - 1)
            inc(t.transpose(PS[b][:, 0:8], BSTG[0:8, i * 128:(i + 1) * 128], ID[0:8, 0:8]), "mm", 1)

    @block.vector
    def _(v):
        for i in range(len(VPAR)):
            b = 4 + (i % 2)
            v.wait_ge(sem["mm"], mm_b0 + i + 1)
            v.tensor_copy(BIAS[:, i, :], PS[b][:, 0:8])
            v.drain()
            inc(v.nop(), "evv", 1)
    evv_bias = cnt["evv"]

    @block.scalar
    def _(s):
        s.wait_ge(sem["evv"], evv_bias)
        s.activation(SA[:, 0, :], BIAS[:, VIDX["alog_f"], :], AF.Sigmoid)
        s.activation(SA[:, 1, :], BIAS[:, VIDX["alog_b"], :], AF.Sigmoid)
        s.drain()
        inc(s.nop(), "eva", 1)
    eva_setup = cnt["eva"]

    if level <= 0:
        @block.sync
        def _(sync):
            join(sync, snap(), only=("dg", "ds", "ldc"))
            sync.wait_ge(sem["eva"], eva_setup)
        stack.close()
        return nc

    arenaA = contextlib.ExitStack()
    arenaPA = contextlib.ExitStack()

    def A(nm, shp, dt):
        return arenaA.enter_context(nc.sbuf_tensor(nm, shp, dt))

    def APA(nm, shp, dt):
        return arenaPA.enter_context(nc.sbuf_tensor(nm, shp, dt, side="right"))

    PAX = APA("PAX", [128, 2, D], F32)
    PAXN = APA("PAXN", [128, 2, D], BF16)
    PAST = APA("PAST", [128, 2, 2, 6], F32)
    PAMV = APA("PAMV", [128, 2, 2], F32)
    PARE = APA("PARE", [128, 2], F32)
    PARQ = APA("PARQ", [128, 2], F32)
    PARS = APA("PARS", [128, 2], F32)
    XNT2 = APA("XNT2", [128, KD, SEQ], BF16)
    LGSB = APA("LGSB", [8, SEQ], F32)
    LT = A("LT", [128, NT, 8], F32)
    TOP8 = A("TOP8", [128, NT, 8], F32)
    IDX8 = A("IDX8", [128, NT, 8], U32)
    I1F = A("I1F", [128, NT], F32)
    I2F = A("I2F", [128, NT], F32)
    G1T = A("G1T", [128, NT], F32)
    G2T = A("G2T", [128, NT], F32)
    FONE = A("FONE", [128, NT * 8], F32)
    CUMT = A("CUMT", [128, NT * 8], F32)
    CUM2 = A("CUM2", [128, NT * 8], F32)
    TROW = A("TROW", [1, 2, NT * 8], F32)
    OFFT = A("OFFT", [1, 2, 8, 65], F32)
    COPS = A("COPS", [1, 64], F32)
    TINS = A("TINS", [1, 16], F32)
    TOTSB = A("TOTSB", [8, 16], F32)
    PTMP = A("PTMP", [128, NT], F32)
    PTM2 = A("PTM2", [128, NT], F32)
    PTM3 = A("PTM3", [128, NT], F32)
    UTMP = A("UTMP", [128, NT], U32)
    UTM2 = A("UTM2", [128, NT], U32)
    UTM3 = A("UTM3", [128, NT], U32)

    def finish(arenas, dumps):
        if debug and dumps:
            sn = snap()

            @block.gpsimd
            def _(g):
                join(g, sn)
                for key, src in dumps:
                    inc(g.dma_start(out=dbg[key], in_=src), "sto", 16)

        @block.sync
        def _(sync):
            if cnt["sto"] > 0:
                sync.wait_ge(sem["sto"], cnt["sto"])
            join(sync, snap(), only=("dg", "ds", "cc", "pa"))
        for a in arenas:
            a.close()
        stack.close()
        return nc


    evvA = cnt["evv"]
    evaA = cnt["eva"]
    dgA = cnt["dg"]

    @block.sync
    def _(sync):
        for g_ in range(NT):
            par = g_ % 2
            if g_ >= 2:
                sync.wait_ge(sem["evv"], evvA + 2 * (g_ - 2) + 2)
            inc(sync.dma_start(out=PAX[:, par, :], in_=x_my[g_ * 128:(g_ + 1) * 128, :]),
                f"pa{par}", 16)

    @block.vector
    def _(v):
        for g_ in range(NT):
            par = g_ % 2
            v.wait_ge(sem[f"pa{par}"], 16 * (g_ // 2 + 1))
            v.bn_stats(PAST[:, par, 0, :], PAX[:, par, 0:512])
            v.bn_stats(PAST[:, par, 1, :], PAX[:, par, 512:1024])
            v.drain()
            v.bn_aggr(PAMV[:, par, :], PAST[:, par, :, :])
            v.drain()
            v.tensor_scalar(PARE[:, par:par + 1], PAMV[:, par, 1:2], EPS, None, ALU.add)
            v.drain()
            if g_ >= 2:
                v.wait_ge(sem["eva"], evaA + g_ - 1)
            inc(v.reciprocal(PARQ[:, par:par + 1], PARE[:, par:par + 1]), "evv", 1)
            v.wait_ge(sem["eva"], evaA + g_ + 1)
            if g_ >= 2:
                v.wait_ge(sem[f"xs{par}"], 16 * ((g_ - 2) // 2 + 1))
            inc(v.tensor_scalar(PAXN[:, par, :], PAX[:, par, :], PAMV[:, par, 0:1],
                                PARS[:, par:par + 1], ALU.subtract, ALU.mult), "evv", 1)

    @block.scalar
    def _(s):
        for g_ in range(NT):
            par = g_ % 2
            s.wait_ge(sem["evv"], evvA + 2 * g_ + 1)
            inc(s.activation(PARS[:, par:par + 1], PARQ[:, par:par + 1], AF.Sqrt), "eva", 1)

    @block.gpsimd
    def _(g):
        for g_ in range(NT):
            par = g_ % 2
            g.wait_ge(sem["evv"], evvA + 2 * g_ + 2)
            inc(g.dma_start(out=xn_bf[g_ * 128:(g_ + 1) * 128, :], in_=PAXN[:, par, :]),
                f"xs{par}", 16)
            if (g_ + 1) % 16 == 0:
                s_ = (g_ + 1) // 16 - 1
                g.wait_ge(sem["xs0"], cnt["xs0"])
                g.wait_ge(sem["xs1"], cnt["xs1"])
                inc(g.nop(), "xnf", 1)
                if level > 0.5:
                    inc(g.collective_compute(
                        "AllGather", ALU.bypass, replica_groups=RG,
                        ins=[xn_bf[s_ * SEQ:(s_ + 1) * SEQ, :]],
                        outs=[xn_all[s_ * 8 * SEQ:(s_ + 1) * 8 * SEQ, :]]), "cc", 1)
    cc_xnall = cnt["cc"]

    if level <= 0.7:
        return finish([arenaA], [("xn", xn_bf)])

    paL = cnt["pa"]
    mmL = cnt["mm"]
    evvL = cnt["evv"]

    @block.sync
    def _(sync):
        for s_ in range(NSEQ):
            sync.wait_ge(sem["xnf"], s_ + 1)
            if s_ >= 1:
                sync.wait_ge(sem["mm"], mmL + 20 * s_)
            for k in range(KD):
                inc(sync.dma_start(out=XNT2[:, k, :],
                                   in_=xn_bf[s_ * SEQ:(s_ + 1) * SEQ, k * 128:(k + 1) * 128],
                                   transpose=True), "pa", 16)

    @block.tensor
    def _(t):
        t.wait_ge(sem["ldc"], ldc0)
        t.wait_ge(sem["ldg"], ldg_all)
        t.wait_ge(sem["eva"], eva_setup)
        for s_ in range(NSEQ):
            base_mm = mmL + 20 * s_
            base_ev = evvL + 20 * s_
            t.wait_ge(sem["pa"], paL + 16 * 8 * (s_ + 1))
            for nb in range(4):
                b = 4 + (nb % 2)
                if nb >= 2:
                    t.wait_ge(sem["evv"], base_ev + nb - 1)
                elif s_ >= 1:
                    t.wait_ge(sem["evv"], base_ev)
                ins = None
                for k in range(KD):
                    ins = t.matmul(PS[b][0:8, :], WGSB[:, k, :],
                                   XNT2[:, k, nb * 512:(nb + 1) * 512],
                                   start=(k == 0), stop=(k == KD - 1))
                inc(ins, "mm", 1)
            for t16 in range(16):
                b = 4 + (t16 % 2)
                t.wait_ge(sem["evv"], base_ev + 4)
                if t16 >= 2:
                    t.wait_ge(sem["evv"], base_ev + 4 + t16 - 1)
                inc(t.transpose(PS[b][:, 0:8], LGSB[0:8, t16 * 128:(t16 + 1) * 128],
                                ID[0:8, 0:8]), "mm", 1)

    @block.vector
    def _(v):
        for s_ in range(NSEQ):
            base_mm = mmL + 20 * s_
            for nb in range(4):
                v.wait_ge(sem["mm"], base_mm + nb + 1)
                b = 4 + (nb % 2)
                inc(v.tensor_copy(LGSB[0:8, nb * 512:(nb + 1) * 512], PS[b][0:8, :]), "evv", 1)
            for t16 in range(16):
                v.wait_ge(sem["mm"], base_mm + 4 + t16 + 1)
                b = 4 + (t16 % 2)
                inc(v.tensor_copy(LT[:, s_ * 16 + t16, :], PS[b][:, 0:8]), "evv", 1)
    evv_lt = cnt["evv"]
    mm_lt = cnt["mm"]
    arenaPA.close()

    if level <= 1:
        return finish([arenaA], [("xn", xn_bf),
                                 ("lt", LT[:].rearrange("p t e -> p (t e)"))])

    @block.vector
    def _(v):
        v.wait_ge(sem["evv"], evv_lt)
        v.drain()
        for t_ in range(NT):
            v.max(out=TOP8[:, t_, :], in_=LT[:, t_, :])
        v.drain()
        for t_ in range(NT):
            v.max_index(IDX8[:, t_, :], TOP8[:, t_, :], LT[:, t_, :])
        v.drain()
        v.tensor_copy(I1F[:], IDX8[:, :, 0])
        v.tensor_copy(I2F[:], IDX8[:, :, 1])
        v.drain()
        v.tensor_sub(PTMP[:], TOP8[:, :, 0], TOP8[:, :, 1])
        v.drain()
        inc(v.nop(), "evv", 1)
    evv_r1 = cnt["evv"]

    @block.scalar
    def _(s):
        s.wait_ge(sem["evv"], evv_r1)
        s.activation(G1T[:], PTMP[:], AF.Sigmoid)
        s.activation(G2T[:], PTMP[:], AF.Sigmoid, scale=-1.0)
        s.drain()
        inc(s.nop(), "eva", 1)
    eva_g12 = cnt["eva"]

    mm_tot = {}
    evv_tot = {}
    for kq, i_f in ((0, I1F), (1, I2F)):
        @block.vector
        def _(v, kq=kq, i_f=i_f):
            if kq == 0:
                v.wait_ge(sem["evv"], evv_r1)
            else:
                v.wait_ge(sem["mm"], mm_tot[0])
            for e_ in range(E):
                v.tensor_scalar(FONE[:].rearrange("p (t e) -> p t e", e=8)[:, :, e_],
                                i_f[:], float(e_), None, ALU.is_equal)
            v.drain()
            inc(v.nop(), "evv", 1)
        evv_f = cnt["evv"]

        @block.tensor
        def _(t, kq=kq, evv_f=evv_f):
            t.wait_ge(sem["evv"], evv_f)
            if kq == 1:
                t.wait_ge(sem["evv"], evv_tot[0])
            b = 2 + kq
            inc(t.matmul(PS[b][0:1, :], U128[:, 127:128], FONE[:], start=True, stop=True),
                "mm", 1)
        mm_tot[kq] = cnt["mm"]

        @block.vector
        def _(v, kq=kq):
            v.wait_ge(sem["mm"], mm_tot[kq])
            v.tensor_copy(TROW[0:1, kq, :], PS[2 + kq][0:1, :])
            v.drain()
            v.tensor_reduce(TINS[0:1, kq * 8:(kq + 1) * 8],
                            TROW[0:1, kq, :].rearrange("p (t e) -> p e t", e=8),
                            AX.X, ALU.add)
            v.drain()
            inc(v.nop(), "evv", 1)
        evv_tot[kq] = cnt["evv"]

    @block.gpsimd
    def _(g):
        g.wait_ge(sem["evv"], evv_tot[1])
        inc(g.dma_start(out=tots_in, in_=TINS[:]), "dg", 16)
        g.wait_ge(sem["dg"], cnt["dg"])
        inc(g.collective_compute("AllGather", ALU.bypass, replica_groups=RG,
                                 ins=[tots_in], outs=[tots_out]), "cc", 1)
        g.wait_ge(sem["cc"], cnt["cc"])
        inc(g.dma_start(out=TOTSB[:], in_=tots_out), "dg", 16)
    dg_tots = cnt["dg"]

    mm_co = cnt["mm"]

    @block.tensor
    def _(t):
        t.wait_ge(sem["dg"], dg_tots)
        t.wait_ge(sem["evv"], evv_tot[1])
        t.matmul(PS[2][0:1, 0:16], MASK2[:, 0:1], TOTSB[:], start=True, stop=True)
        inc(t.matmul(PS[2][0:1, 16:32], MASK2[:, 1:2], TOTSB[:], start=True, stop=True),
            "mm", 1)

    @block.vector
    def _(v):
        v.wait_ge(sem["mm"], mm_co + 1)
        v.tensor_copy(COPS[0:1, 0:32], PS[2][0:1, 0:32])
        v.drain()
        v.tensor_scalar(COPS[0:1, 32:40], COPS[0:1, 0:8], -1.0, None, ALU.add)
        v.tensor_add(COPS[0:1, 48:56], COPS[0:1, 16:24], COPS[0:1, 8:16])
        v.drain()
        v.tensor_scalar(COPS[0:1, 40:48], COPS[0:1, 48:56], -1.0, None, ALU.add)
        v.drain()
        for kq in range(2):
            for e_ in range(E):
                ci = COPS[0:1, 32 + kq * 8 + e_:33 + kq * 8 + e_]
                v.tensor_copy(OFFT[0:1, kq, e_, 0:1], ci)
                v.tensor_tensor_scan(OFFT[0:1, kq, e_, 1:65],
                                     TROW[0:1, kq, :].rearrange("p (t e) -> p t e", e=8)[:, :, e_],
                                     ZERO[0:1, 0:64], ci, ALU.add, ALU.add)
        v.drain()
        inc(v.nop(), "evv", 1)
    evv_off = cnt["evv"]

    for kq, i_f, gsrc, TSL, GOUT, YOUT in (
        (0, I1F, G1T, TSLOT0, GATE0, YIDX0),
        (1, I2F, G2T, TSLOT1, GATE1, YIDX1),
    ):
        @block.vector
        def _(v, kq=kq, i_f=i_f):
            v.wait_ge(sem["evv"], evv_off)
            for e_ in range(E):
                v.tensor_scalar(FONE[:].rearrange("p (t e) -> p t e", e=8)[:, :, e_],
                                i_f[:], float(e_), None, ALU.is_equal)
            v.drain()
            inc(v.nop(), "evv", 1)
        evv_oh = cnt["evv"]

        @block.tensor
        def _(t, kq=kq, evv_oh=evv_oh):
            t.wait_ge(sem["evv"], evv_oh)
            b = 2 + kq
            toff_view = OFFT[0:1, kq, :, 0:64].rearrange("p e t -> p t e")
            t.matmul(PS[b][:], U128[:], FONE[:], start=True, stop=False)
            inc(t.matmul(PS[b][:], ONESROW[:], toff_view, start=False, stop=True), "mm", 1)
        mm_cum = cnt["mm"]

        @block.scalar
        def _(s, kq=kq, mm_cum=mm_cum):
            s.wait_ge(sem["mm"], mm_cum)
            s.copy(CUMT[:], PS[2 + kq][:])
            s.drain()
            inc(s.nop(), "eva", 1)
        eva_cum = cnt["eva"]

        @block.vector
        def _(v, kq=kq, i_f=i_f, gsrc=gsrc, TSL=TSL, GOUT=GOUT, YOUT=YOUT, eva_cum=eva_cum):
            v.wait_ge(sem["eva"], eva_cum)
            v.tensor_mul(CUM2[:], CUMT[:], FONE[:])
            v.drain()
            v.tensor_reduce(PTMP[:], CUM2[:].rearrange("p (t e) -> p t e", e=8), AX.X, ALU.add)
            v.drain()
            v.tensor_scalar(PTM2[:], PTMP[:], float(CAP), None, ALU.is_lt)
            v.drain()
            v.tensor_mul(GOUT[:], gsrc[:], PTM2[:])
            v.tensor_scalar(PTM3[:], PTMP[:], float(CAP), None, ALU.min)
            v.drain()
            v.scalar_tensor_tensor(PTM3[:], i_f[:], float(TRB), PTM3[:], ALU.mult, ALU.add)
            v.drain()
            v.tensor_copy(TSL[:], PTM3[:])
            v.tensor_mul(PTMP[:], PTMP[:], PTM2[:])
            v.drain()
            v.tensor_copy(UTMP[:], PTMP[:])
            v.tensor_copy(UTM2[:], i_f[:])
            v.drain()
            v.tensor_scalar(UTM3[:], UTMP[:], 11, None, ALU.logical_shift_right)
            v.drain()
            v.tensor_scalar(UTM3[:], UTM3[:], 14, None, ALU.logical_shift_left)
            v.tensor_scalar(UTMP[:], UTMP[:], 2047, None, ALU.bitwise_and)
            v.tensor_scalar(UTM2[:], UTM2[:], 11, None, ALU.logical_shift_left)
            v.drain()
            v.tensor_add(UTM3[:], UTM3[:], UTMP[:])
            v.drain()
            v.tensor_add(YOUT[:], UTM3[:], UTM2[:])
            v.drain()
            inc(v.nop(), "evv", 1)
    evv_route = cnt["evv"]

    @block.gpsimd
    def _(g):
        g.wait_ge(sem["evv"], evv_route)
        g.wait_ge(sem["ldc"], ldc0)
        g.wait_ge(sem["ldg"], ldg_all)
        for TSL in (TSLOT0, TSLOT1):
            for t_ in range(NT):
                g.wait_ge(sem["dg"], cnt["dg"])
                inc(g.indirect_dma_start(
                    out=t_full, out_offset=IndirectOffsetOnAxis(ap=TSL[:, t_:t_ + 1], axis=0),
                    in_=TOKP[:, 2 * t_:2 * t_ + 2], in_offset=None), "dg", 16)
        g.wait_ge(sem["dg"], cnt["dg"])
        inc(g.collective_compute("ReduceScatter", ALU.add, replica_groups=RG,
                                 ins=[t_full], outs=[t_rs]), "cc", 1)
    cc_trs = cnt["cc"]
    snA_end = snap()

    if level <= 2:
        return finish([arenaA], [("xn", xn_bf), ("trs", t_rs),
                                 ("slot0", TSLOT0[:]), ("slot1", TSLOT1[:]),
                                 ("gate0", GATE0[:]), ("gate1", GATE1[:]),
                                 ("yidx0", YIDX0[:]), ("yidx1", YIDX1[:])])


    arenaD = contextlib.ExitStack()

    def AD(nm, shp, dt):
        return arenaD.enter_context(nc.sbuf_tensor(nm, shp, dt))

    DBUF = AD("DBUF", [128, 2, D], BF16)
    TSBF = AD("TSBF", [128, 2 * 128], F32)
    TTM1 = AD("TTM1", [128, 2 * 128], F32)
    TTM2 = AD("TTM2", [128, 2 * 128], F32)

    @block.gpsimd
    def _(g):
        g.wait_ge(sem["cc"], cc_trs)
        join(g, snA_end, only=("evv", "eva", "mm", "pa"))
        inc(g.dma_start(out=TSBF[:], in_=t_rs[0:CAP, :].rearrange("(p t) k -> p (t k)", p=128)),
            "dg", 16)
    dg_tsb = cnt["dg"]

    @block.vector
    def _(v):
        v.wait_ge(sem["dg"], dg_tsb)
        v.tensor_scalar(TTM1[:], TSBF[:], 0.0, None, ALU.is_equal)
        v.drain()
        v.scalar_tensor_tensor(TTM2[:], TTM1[:], 65537.0, TSBF[:], ALU.mult, ALU.add)
        v.drain()
        v.tensor_scalar(TTM1[:], TTM2[:], -1.0, None, ALU.add)
        v.drain()
        v.tensor_copy(XIDX[:], TTM1[:])
        v.drain()
        inc(v.nop(), "evv", 1)
    evv_xidx = cnt["evv"]

    dgD2 = cnt["dg"]
    dmyv = d_my.rearrange("(p t) dd -> p t dd", t=128)

    @block.gpsimd
    def _(g):
        g.wait_ge(sem["evv"], evv_xidx)
        for t_ in range(128):
            par = t_ % 2
            if t_ >= 1:
                g.wait_ge(sem["dg"], cnt["dg"])
            inc(g.indirect_dma_start(
                out=DBUF[:, par, :], out_offset=None,
                in_=xn_all, in_offset=IndirectOffsetOnAxis(ap=XIDX[:, 2 * t_:2 * t_ + 1], axis=0)),
                "dg", 16)
            if t_ >= 1:
                pw = (t_ - 1) % 2
                inc(g.dma_start(out=dmyv[:, t_ - 1, :], in_=DBUF[:, pw, :]), "dg", 16)
        g.wait_ge(sem["dg"], cnt["dg"])
        inc(g.dma_start(out=dmyv[:, 127, :], in_=DBUF[:, 1, :]), "dg", 16)
    dg_dmy = cnt["dg"]

    if level <= 3:
        return finish([arenaD], [("xn", xn_bf), ("trs", t_rs), ("dmy", d_my),
                                 ("slot0", TSLOT0[:]), ("slot1", TSLOT1[:]),
                                 ("gate0", GATE0[:]), ("gate1", GATE1[:]),
                                 ("yidx0", YIDX0[:]), ("yidx1", YIDX1[:])])

    snD_end = snap()

    wstate = {"iw": 0}

    def emit_mixer_batch(seqs):
        batch_tag = f"_b{seqs[0]}"
        arenaM = contextlib.ExitStack()

        def AM(nm, shp, dt):
            return arenaM.enter_context(nc.sbuf_tensor(nm + batch_tag, shp, dt))

        WBUF = AM("WBUF", [128, 2, KD, D], BF16)
        BUF0 = AM("BUF0", [128, KD, HALF], BF16)
        BUF1 = AM("BUF1", [128, KD, HALF], BF16)
        BUFA = AM("BUFA", [128, KD, HALF], F32)
        BUFYF = AM("BUFYF", [128, KD, SEQ], BF16)
        XNTH = AM("XNTH", [128, KD, HALF], BF16)
        XGC = AM("XGC", [128, KD, 512], BF16)
        GXC = AM("GXC", [128, 512], BF16)
        GAC = AM("GAC", [128, 512], BF16)
        RC = AM("RC", [128, 512], F32)
        T1C = AM("T1C", [128, 512], BF16)
        TMU = AM("TMU", [128, 512], F32)
        SST = AM("SST", [128, 4, 512], BF16)

        mstM0 = cnt["mst"]
        state = wstate
        batch_i0 = state["iw"]

        def load_weight(widx):
            i = state["iw"]
            j = i - batch_i0
            buf = i % 2

            @block.sync
            def _(sync):
                if j < 2:
                    sync.wait_ge(sem["ldg"], ldg_all)
                    sync.wait_ge(sem["mst"], mstM0)
                    sync.wait_ge(sem["bka"], cnt["bka"])
                    sync.wait_ge(sem["bkv"], cnt["bkv"])
                    sync.wait_ge(sem["ys0"], cnt["ys0"])
                    sync.wait_ge(sem["ys1"], cnt["ys1"])
                else:
                    sync.wait_ge(sem["mst"], mstM0 + j - 1)
                inc(sync.dma_start(out=WBUF[:, buf, :, :],
                                   in_=wbf[widx].rearrange("(k p) m -> p k m", p=128)),
                    f"wl{i % 2}", 16)
            state["iw"] = i + 1
            return buf, (f"wl{i % 2}", 16 * (i // 2 + 1))

        def half_tokens(s_, h_):
            return s_ * SEQ + h_ * HALF

        def emit_xnt_load(s_, h_, first=False):
            sn = snap()

            @block.sync
            def _(sync):
                if first and batch_i0 == 0:
                    sync.wait_ge(sem["mm"], mm_lt)
                    sync.wait_ge(sem["evv"], evv_lt)
                    sync.wait_ge(sem["xs0"], cnt["xs0"])
                    sync.wait_ge(sem["xs1"], cnt["xs1"])
                else:
                    join(sync, sn, only=("bka", "bkv", "mst", "eva", "evv", "xs0", "xs1",
                                         "ys0", "ys1"))
                base = half_tokens(s_, h_)
                for k in range(KD):
                    inc(sync.dma_start(out=XNTH[:, k, :],
                                       in_=xn_bf[base:base + HALF, k * 128:(k + 1) * 128],
                                       transpose=True), "pa", 16)

        def chunk_phys(buf, k, nb, rev):
            if not rev:
                return buf[:, k, nb * 512:(nb + 1) * 512]
            return buf[:, k, HALF - (nb + 1) * 512:HALF - nb * 512]

        def chunk_rhs(buf, k, nb, rev):
            c = chunk_phys(buf, k, nb, rev)
            return c[:, ::-1] if rev else c

        def emit_stage_xhat(s_, h_, d_):
            rev = (d_ == 1)
            buf, wtgt = load_weight(0 if d_ == 0 else 4)
            mmS = cnt["mm"]
            bkaS = cnt["bka"]
            bkvS = cnt["bkv"]
            paS = cnt["pa"]
            bxname = "bx_f" if d_ == 0 else "bx_b"

            def bka_xg(nb):
                return bkaS + (1 if nb == 0 else 10)

            def bka_ev(gi):
                nb = gi // 8
                return bkaS + gi + (2 if nb == 0 else 3)

            @block.scalar
            def _(s):
                for nb in range(2):
                    if nb == 0:
                        s.wait_ge(sem["pa"], paS)
                    else:
                        s.wait_ge(sem["mm"], mmS + 8)
                    ins = None
                    for k in range(KD):
                        ins = s.activation(XGC[:, k, :], chunk_phys(XNTH, k, nb, rev),
                                           AF.Gelu_apprx_tanh)
                    inc(ins, "bka", 1)
                    for m in range(KD):
                        gi = nb * 8 + m
                        s.wait_ge(sem["mm"], mmS + gi + 1)
                        b = gi % 4
                        inc(s.add(BUF0[:, m, nb * 512:(nb + 1) * 512], PS[b][:],
                                  bias_col(bxname)[:, m:m + 1]), "bka", 1)

            @block.tensor
            def _(t):
                t.wait_ge(sem[wtgt[0]], wtgt[1])
                t.wait_ge(sem["bka"], bkaS)
                t.wait_ge(sem["bkv"], bkvS)
                for nb in range(2):
                    t.wait_ge(sem["bka"], bka_xg(nb))
                    for m in range(KD):
                        gi = nb * 8 + m
                        b = gi % 4
                        if gi >= 4:
                            t.wait_ge(sem["bka"], bka_ev(gi - 4))
                        ins = None
                        for k in range(KD):
                            ins = t.matmul(PS[b][:], WBUF[:, buf, k, m * 128:(m + 1) * 128],
                                     chunk_rhs(XGC, k, 0, False) if False else
                                     (XGC[:, k, :][:, ::-1] if rev else XGC[:, k, :]),
                                     start=(k == 0), stop=(k == KD - 1))
                        inc(ins, "mm", 1)
                inc(t.nop(), "mst", 1)

        def emit_stage_gates(s_, h_, d_):
            bufx, wtx = load_weight(1 if d_ == 0 else 5)
            bufa_, wta = load_weight(2 if d_ == 0 else 6)
            mmS = cnt["mm"]
            bkaS = cnt["bka"]
            bkvS = cnt["bkv"]
            gxn = "bgx_f" if d_ == 0 else "bgx_b"
            gan = "bga_f" if d_ == 0 else "bga_b"

            def mm_at(gi, w):
                return mmS + 2 * gi + w + 1

            def bka_at(gi, w):
                return bkaS + 3 * gi + w + 1

            def bkv_at(gi, w):
                return bkvS + 2 * gi + w + 1

            @block.tensor
            def _(t):
                t.wait_ge(sem[wtx[0]], wtx[1])
                t.wait_ge(sem[wta[0]], wta[1])
                t.wait_ge(sem["bka"], bkaS)
                t.wait_ge(sem["bkv"], bkvS)
                for gi in range(16):
                    nb, m = divmod(gi, 8)
                    bA = (2 * gi) % 4
                    bB = (2 * gi + 1) % 4
                    if gi >= 2:
                        t.wait_ge(sem["bka"], bka_at(gi - 2, 0))
                    ins = None
                    for k in range(KD):
                        ins = t.matmul(PS[bA][:], WBUF[:, bufx, k, m * 128:(m + 1) * 128],
                                 BUF0[:, k, nb * 512:(nb + 1) * 512],
                                 start=(k == 0), stop=(k == KD - 1))
                    inc(ins, "mm", 1)
                    if gi >= 2:
                        t.wait_ge(sem["bka"], bka_at(gi - 2, 1))
                    ins = None
                    for k in range(KD):
                        ins = t.matmul(PS[bB][:], WBUF[:, bufa_, k, m * 128:(m + 1) * 128],
                                 BUF0[:, k, nb * 512:(nb + 1) * 512],
                                 start=(k == 0), stop=(k == KD - 1))
                    inc(ins, "mm", 1)
                inc(t.nop(), "mst", 1)
                inc(t.nop(), "mst", 1)

            @block.scalar
            def _(s):
                for gi in range(16):
                    nb, m = divmod(gi, 8)
                    bA = (2 * gi) % 4
                    bB = (2 * gi + 1) % 4
                    s.wait_ge(sem["mm"], mm_at(gi, 0))
                    if gi >= 1:
                        s.wait_ge(sem["bkv"], bkv_at(gi - 1, 0))
                    inc(s.activation(GXC[:], PS[bA][:], AF.Sigmoid,
                                     bias=bias_col(gxn)[:, m:m + 1]), "bka", 1)
                    s.wait_ge(sem["mm"], mm_at(gi, 1))
                    inc(s.activation(GAC[:], PS[bB][:], AF.Sigmoid,
                                     bias=bias_col(gan)[:, m:m + 1]), "bka", 1)
                    s.wait_ge(sem["bkv"], bkv_at(gi, 0))
                    inc(s.activation(RC[:], TMU[:], AF.Sqrt, bias=C1[:], scale=-1.0), "bka", 1)

            @block.vector
            def _(v):
                for gi in range(16):
                    nb, m = divmod(gi, 8)
                    v.wait_ge(sem["bka"], bka_at(gi, 1))
                    v.tensor_scalar(BUFA[:, m, nb * 512:(nb + 1) * 512], GAC[:],
                                    SA[:, d_, m:m + 1], None, ALU.mult)
                    v.tensor_tensor(T1C[:], GXC[:], BUF0[:, m, nb * 512:(nb + 1) * 512], ALU.mult)
                    v.drain()
                    inc(v.tensor_tensor(TMU[:], BUFA[:, m, nb * 512:(nb + 1) * 512],
                                        BUFA[:, m, nb * 512:(nb + 1) * 512], ALU.mult), "bkv", 1)
                    v.wait_ge(sem["bka"], bka_at(gi, 2))
                    inc(v.tensor_tensor(BUF1[:, m, nb * 512:(nb + 1) * 512], RC[:], T1C[:],
                                        ALU.mult), "bkv", 1)

        def emit_stage_scan(s_, h_, d_, first_half_of_dir):
            mst_t = cnt["mst"]

            @block.vector
            def _(v):
                v.wait_ge(sem["mst"], mst_t)
                v.drain()
                for m in range(KD):
                    init = 0.0 if first_half_of_dir else CARRY[:, m:m + 1]
                    v.tensor_tensor_scan(BUF0[:, m, :], BUFA[:, m, :], BUF1[:, m, :],
                                         init, ALU.mult, ALU.add)
                    v.drain()
                    v.tensor_copy(CARRY[:, m:m + 1], BUF0[:, m, HALF - 1:HALF])
                    v.drain()
                inc(v.nop(), "bkv", 1)
            return cnt["bkv"]

        def emit_stage_wo_fwd(s_, h_, scan_done):
            buf, wtgt = load_weight(3)
            mmS = cnt["mm"]
            bkaS = cnt["bka"]
            bkvS = cnt["bkv"]

            @block.tensor
            def _(t):
                t.wait_ge(sem[wtgt[0]], wtgt[1])
                t.wait_ge(sem["bka"], bkaS)
                t.wait_ge(sem["bkv"], scan_done)
                for gi in range(16):
                    nb, m = divmod(gi, 8)
                    b = gi % 4
                    if gi >= 4:
                        t.wait_ge(sem["bka"], bkaS + gi - 4 + 1)
                    ins = None
                    for k in range(KD):
                        ins = t.matmul(PS[b][:], WBUF[:, buf, k, m * 128:(m + 1) * 128],
                                 BUF0[:, k, nb * 512:(nb + 1) * 512],
                                 start=(k == 0), stop=(k == KD - 1))
                    inc(ins, "mm", 1)
                inc(t.nop(), "mst", 1)

            @block.scalar
            def _(s):
                for gi in range(16):
                    nb, m = divmod(gi, 8)
                    b = gi % 4
                    s.wait_ge(sem["mm"], mmS + gi + 1)
                    col = h_ * HALF + nb * 512
                    inc(s.add(BUFYF[:, m, col:col + 512], PS[b][:],
                              bias_col("bo_f")[:, m:m + 1]), "bka", 1)

        def emit_stage_wo_bwd(s_, h_, scan_done):
            buf, wtgt = load_weight(7)
            mmS = cnt["mm"]
            bkvS = cnt["bkv"]
            bkaS = cnt["bka"]

            @block.tensor
            def _(t):
                t.wait_ge(sem[wtgt[0]], wtgt[1])
                t.wait_ge(sem["bka"], bkaS)
                t.wait_ge(sem["bkv"], scan_done)
                for gi in range(16):
                    nb, m = divmod(gi, 8)
                    b = gi % 4
                    if gi >= 4:
                        t.wait_ge(sem["bkv"], bkvS + gi - 4 + 1)
                    ins = None
                    for k in range(KD):
                        ins = t.matmul(PS[b][:], WBUF[:, buf, k, m * 128:(m + 1) * 128],
                                 chunk_rhs(BUF0, k, nb, True),
                                 start=(k == 0), stop=(k == KD - 1))
                    inc(ins, "mm", 1)
                inc(t.nop(), "mst", 1)

            @block.vector
            def _(v):
                for gi in range(16):
                    nb, m = divmod(gi, 8)
                    b = gi % 4
                    v.wait_ge(sem["mm"], mmS + gi + 1)
                    if gi >= 1:
                        v.wait_ge(sem["bka"], bkaS + gi)
                    col = h_ * HALF + nb * 512
                    v.scalar_tensor_tensor(TMU[:], PS[b][:], bias_col("bo_b")[:, m:m + 1],
                                           BUFYF[:, m, col:col + 512], ALU.add, ALU.add)
                    v.drain()
                    inc(v.scalar_tensor_tensor(RC[:], TMU[:], 0.5,
                                               XNTH[:, m, nb * 512:(nb + 1) * 512],
                                               ALU.mult, ALU.add), "bkv", 1)

            @block.scalar
            def _(s):
                for gi in range(16):
                    nb, m = divmod(gi, 8)
                    s.wait_ge(sem["bkv"], bkvS + gi + 1)
                    inc(s.activation(BUF1[:, m, nb * 512:(nb + 1) * 512], RC[:],
                                     AF.Gelu_apprx_tanh), "bka", 1)

        def emit_stage_ld(s_, h_):
            buf, wtgt = load_weight(8)
            mmS = cnt["mm"]
            bkvS = cnt["bkv"]
            bkaS = cnt["bka"]
            dsS = cnt["ds"]
            base = half_tokens(s_, h_)
            stv = s_t.rearrange("(mm p) tok -> p mm tok", p=128)

            @block.tensor
            def _(t):
                t.wait_ge(sem[wtgt[0]], wtgt[1])
                t.wait_ge(sem["bka"], bkaS)
                t.wait_ge(sem["bkv"], bkvS)
                for gi in range(16):
                    nb, m = divmod(gi, 8)
                    b = gi % 4
                    if gi >= 4:
                        t.wait_ge(sem["bkv"], bkvS + gi - 4 + 1)
                    ins = None
                    for k in range(KD):
                        ins = t.matmul(PS[b][:], WBUF[:, buf, k, m * 128:(m + 1) * 128],
                                 BUF1[:, k, nb * 512:(nb + 1) * 512],
                                 start=(k == 0), stop=(k == KD - 1))
                    inc(ins, "mm", 1)
                inc(t.nop(), "mst", 1)

            @block.vector
            def _(v):
                v.wait_ge(sem["ds"], dsS)
                for gi in range(16):
                    nb, m = divmod(gi, 8)
                    b = gi % 4
                    v.wait_ge(sem["mm"], mmS + gi + 1)
                    if gi >= 4:
                        v.wait_ge(sem["ds"], dsS + 16 * (gi // 4))
                    inc(v.scalar_tensor_tensor(SST[:, m % 4, :], PS[b][:],
                                               bias_col("ld_b")[:, m:m + 1],
                                               XNTH[:, m, nb * 512:(nb + 1) * 512],
                                               ALU.add, ALU.add), "bkv", 1)

            @block.sync
            def _(sync):
                for q in range(4):
                    nb, half_m = divmod(q, 2)
                    sync.wait_ge(sem["bkv"], bkvS + 4 * (q + 1))
                    col = base + nb * 512
                    inc(sync.dma_start(out=stv[:, half_m * 4:(half_m + 1) * 4, col:col + 512],
                                       in_=SST[:]), "ds", 16)

        def emit_half(s_, h_, d_, first_half_of_dir, load_xnt, first=False):
            if load_xnt:
                emit_xnt_load(s_, h_, first=first)
            emit_stage_xhat(s_, h_, d_)
            emit_stage_gates(s_, h_, d_)
            sd = emit_stage_scan(s_, h_, d_, first_half_of_dir)
            if d_ == 0:
                emit_stage_wo_fwd(s_, h_, sd)
            else:
                emit_stage_wo_bwd(s_, h_, sd)
                emit_stage_ld(s_, h_)

        first_seq = True
        for s_ in seqs:
            emit_half(s_, 0, 0, True, True, first=first_seq)
            first_seq = False
            emit_half(s_, 1, 0, False, True)
            emit_half(s_, 1, 1, True, False)
            emit_half(s_, 0, 1, False, True)

        arenaM.close()

    emit_mixer_batch([0, 1])

    arenaE = contextlib.ExitStack()

    def AE(nm, shp, dt):
        return arenaE.enter_context(nc.sbuf_tensor(nm, shp, dt))

    W1SB = AE("W1SB", [128, KD, H], BF16)
    W2SB = AE("W2SB", [128, KH, D], BF16)
    HT = AE("HT", [128, KH, CBL], BF16)
    DTB = AE("DTB", [128, 2, KD, CBL], BF16)
    YSB = AE("YSB", [128, 2, D], BF16)
    B2BC = AE("B2BC", [128, D], BF16)
    B2ROW = AE("B2ROW", [1, D], F32)
    B1STG = AE("B1STG", [32, 128], F32)
    B1SB = AE("B1SB", [128, KH], F32)

    snEL = snap()

    @block.gpsimd
    def _(g):
        join(g, snEL, only=("evv", "eva", "dg", "ds", "bka", "bkv", "mst", "pa",
                            "wl0", "wl1", "xs0", "xs1", "ldg"))
        inc(g.dma_start(out=W1SB[:], in_=w1bf.rearrange("(k p) h -> p k h", p=128)), "dg", 16)
        inc(g.dma_start(out=B1STG[:], in_=b1_my.rearrange("(m p) -> m p", p=128)), "dg", 16)
        inc(g.dma_start(out=B2ROW[:], in_=b2_my[None, :]), "dg", 16)

    @block.sync
    def _(sync):
        join(sync, snEL, only=("evv", "eva", "dg", "ds", "bka", "bkv", "mst", "pa",
                               "wl0", "wl1", "xs0", "xs1", "ldg"))
        inc(sync.dma_start(out=W2SB[:], in_=w2bf.rearrange("(k p) dd -> p k dd", p=128)),
            "dg", 16)
    dg_ew = cnt["dg"]

    mmE0 = cnt["mm"]
    evvE0 = cnt["evv"]

    @block.tensor
    def _(t):
        t.wait_ge(sem["dg"], dg_ew)
        t.wait_ge(sem["bka"], cnt["bka"])
        t.wait_ge(sem["bkv"], cnt["bkv"])
        inc(t.transpose(PS[4][:, 0:KH], B1STG[:], ID[0:32, 0:32]), "mm", 1)
        inc(t.matmul(PS[5][:], ONESROW[:], B2ROW[0:1, 0:512], start=True, stop=True), "mm", 1)
        t.wait_ge(sem["evv"], evvE0 + 1)
        inc(t.matmul(PS[5][:], ONESROW[:], B2ROW[0:1, 512:1024], start=True, stop=True), "mm", 1)

    @block.vector
    def _(v):
        v.wait_ge(sem["mm"], mmE0 + 2)
        v.tensor_copy(B1SB[:], PS[4][:, 0:KH])
        v.tensor_copy(B2BC[:, 0:512], PS[5][:])
        v.drain()
        inc(v.nop(), "evv", 1)
        v.wait_ge(sem["mm"], mmE0 + 3)
        v.tensor_copy(B2BC[:, 512:1024], PS[5][:])
        v.drain()
        inc(v.nop(), "evv", 1)
    evv_eb = cnt["evv"]

    dsE0 = cnt["ds"]
    mmE1 = cnt["mm"]
    mstE0 = cnt["mst"]
    bkaE0 = cnt["bka"]
    bkvE0 = cnt["bkv"]
    dgE1 = cnt["dg"]

    def e_dt(cb):
        return 16 * 8 * (cb // 2 + 1)

    def e_mm_h(cb, m):
        return mmE1 + 36 * cb + m + 1

    def e_mm_y(cb, q):
        return mmE1 + 36 * cb + 32 + q + 1

    def e_bka(cb, m):
        return bkaE0 + 32 * cb + m + 1

    def e_bkv(cb, q):
        return bkvE0 + 4 * cb + q + 1

    def e_mst(cb, stage):
        return mstE0 + 2 * cb + stage



    @block.sync
    def _(sync):
        sync.wait_ge(sem["dg"], dg_dmy)
        for cb in range(NCB):
            par = cb % 2
            if cb >= 2:
                sync.wait_ge(sem["mst"], e_mst(cb - 2, 1))
            for k in range(KD):
                inc(sync.dma_start(out=DTB[:, par, k, :],
                                   in_=d_my[cb * CBL:(cb + 1) * CBL, k * 128:(k + 1) * 128],
                                   transpose=True), f"dt{par}", 16)

    @block.tensor
    def _(t):
        t.wait_ge(sem["evv"], evv_eb)
        t.wait_ge(sem["bka"], cnt["bka"])
        t.wait_ge(sem["bkv"], cnt["bkv"])
        for cb in range(NCB):
            par = cb % 2
            t.wait_ge(sem[f"dt{cb % 2}"], e_dt(cb))
            for m in range(KH):
                b = m % 4
                if m >= 4:
                    t.wait_ge(sem["bka"], e_bka(cb, m - 4))
                elif cb >= 1:
                    t.wait_ge(sem["bka"], e_bka(cb - 1, KH - 4 + m))
                ins = None
                for k in range(KD):
                    ins = t.matmul(PS[b][:, 0:CBL], W1SB[:, k, m * 128:(m + 1) * 128],
                                   DTB[:, par, k, :], start=(k == 0), stop=(k == KD - 1))
                inc(ins, "mm", 1)
            inc(t.nop(), "mst", 1)
            t.wait_ge(sem["bka"], e_bka(cb, KH - 1))
            for q in range(4):
                csub, nd = divmod(q, 2)
                b = 4 + (q % 2)
                if q >= 2:
                    t.wait_ge(sem["bkv"], e_bkv(cb, q - 2))
                elif cb >= 1:
                    t.wait_ge(sem["bkv"], e_bkv(cb - 1, 2 + q))
                ins = None
                for m in range(KH):
                    ins = t.matmul(PS[b][:], HT[:, m, csub * 128:(csub + 1) * 128],
                                   W2SB[:, m, nd * 512:(nd + 1) * 512],
                                   start=(m == 0), stop=(m == KH - 1))
                inc(ins, "mm", 1)
            inc(t.nop(), "mst", 1)

    @block.scalar
    def _(s):
        for cb in range(NCB):
            for m in range(KH):
                s.wait_ge(sem["mm"], e_mm_h(cb, m))
                if cb >= 1 and m == 0:
                    s.wait_ge(sem["mst"], e_mst(cb - 1, 2))
                b = m % 4
                inc(s.activation(HT[:, m, :], PS[b][:, 0:CBL], AF.Relu,
                                 bias=B1SB[:, m:m + 1]), "bka", 1)

    @block.vector
    def _(v):
        for cb in range(NCB):
            for q in range(4):
                csub, nd = divmod(q, 2)
                b = 4 + (q % 2)
                v.wait_ge(sem["mm"], e_mm_y(cb, q))
                if cb >= 1 and nd == 0:
                    v.wait_ge(sem[f"ys{csub}"], 16 * cb)
                inc(v.tensor_tensor(YSB[:, csub, nd * 512:(nd + 1) * 512], PS[b][:],
                                    B2BC[:, nd * 512:(nd + 1) * 512], ALU.add), "bkv", 1)

    @block.gpsimd
    def _(g):
        for cb in range(NCB):
            for csub in range(2):
                g.wait_ge(sem["bkv"], e_bkv(cb, csub * 2 + 1))
                r0 = cb * CBL + csub * 128
                inc(g.dma_start(out=y_my[r0:r0 + 128, :], in_=YSB[:, csub, :]), f"ys{csub}", 16)
            if (cb + 1) % (NCB // NCH) == 0:
                j = (cb + 1) // (NCB // NCH) - 1
                g.wait_ge(sem["ys0"], cnt["ys0"])
                g.wait_ge(sem["ys1"], cnt["ys1"])
                inc(g.collective_compute(
                    "AllGather", ALU.bypass, replica_groups=RG,
                    ins=[y_my[j * YCH:(j + 1) * YCH, :]],
                    outs=[y_all[j * E * YCH:(j + 1) * E * YCH, :]]), "cc", 1)
    cc_yall = cnt["cc"]

    if level <= 4:
        return finish([arenaE], [("xn", xn_bf), ("trs", t_rs), ("dmy", d_my), ("ymy", y_my),
                                 ("slot0", TSLOT0[:]), ("slot1", TSLOT1[:]),
                                 ("gate0", GATE0[:]), ("gate1", GATE1[:]),
                                 ("yidx0", YIDX0[:]), ("yidx1", YIDX1[:])])

    snE_end = snap()
    arenaE.close()

    emit_mixer_batch([2, 3])
    arenaD.close()
    arenaA.close()

    if level <= 5:
        return finish([], [("xn", xn_bf), ("trs", t_rs), ("dmy", d_my), ("ymy", y_my),
                           ("st", s_t),
                           ("slot0", TSLOT0[:]), ("slot1", TSLOT1[:]),
                           ("gate0", GATE0[:]), ("gate1", GATE1[:]),
                           ("yidx0", YIDX0[:]), ("yidx1", YIDX1[:])])

    snM_end = snap()

    arenaC = contextlib.ExitStack()

    def AC(nm, shp, dt):
        return arenaC.enter_context(nc.sbuf_tensor(nm, shp, dt))

    AG0 = AC("AG0", [128, 2, D], BF16)
    AG1 = AC("AG1", [128, 2, D], BF16)
    SCT = AC("SCT", [128, 2, D], BF16)
    XTI = AC("XTI", [128, 2, D], F32)
    CW1 = AC("CW1", [128, 2, D], F32)
    CW2 = AC("CW2", [128, 2, D], F32)
    OUTT = AC("OUTT", [128, 2, D], F32)

    dgC0 = cnt["dg"]
    dsC0 = cnt["ds"]
    evvC0 = cnt["evv"]
    evaC0 = cnt["eva"]
    stoC0 = cnt["sto"]

    @block.gpsimd
    def _(g):
        g.wait_ge(sem["cc"], cc_yall)
        join(g, snM_end, only=("mst", "bka", "bkv", "eva", "evv", "ds"))
        def gather_pair(t_):
            par = t_ % 2
            if t_ >= 2 and combine_mode != 1:
                g.wait_ge(sem["evv"], evvC0 + 2 * (t_ - 2) + 1)
            if plain_gather:
                inc(g.dma_start(out=AG0[:, par, :], in_=y_all[t_ * 128:(t_ + 1) * 128, :]),
                    f"ga{par}", 16)
                inc(g.dma_start(out=AG1[:, par, :], in_=y_all[t_ * 128:(t_ + 1) * 128, :]),
                    f"ga{par}", 16)
            else:
                inc(g.indirect_dma_start(
                    out=AG0[:, par, :], out_offset=None,
                    in_=y_all, in_offset=IndirectOffsetOnAxis(ap=YIDX0[:, t_:t_ + 1], axis=0)),
                    f"ga{par}", 16)
                inc(g.indirect_dma_start(
                    out=AG1[:, par, :], out_offset=None,
                    in_=y_all, in_offset=IndirectOffsetOnAxis(ap=YIDX1[:, t_:t_ + 1], axis=0)),
                    f"ga{par}", 16)

        def store(t_):
            par = t_ % 2
            g.wait_ge(sem["evv"], evvC0 + 2 * t_ + 2)
            inc(g.dma_start(out=out_my[t_ * 128:(t_ + 1) * 128, :], in_=OUTT[:, par, :]),
                f"st{par}", 16)

        for t_ in range(NT):
            gather_pair(t_)
            if combine_mode == 0 and t_ >= 2:
                store(t_ - 2)
        if combine_mode == 0:
            store(NT - 2)
            store(NT - 1)

    @block.sync
    def _(sync):
        join(sync, snM_end, only=("ds", "bkv", "bka", "mst"))
        for t_ in range(NT):
            par = t_ % 2
            if t_ >= 2 and combine_mode == 0:
                sync.wait_ge(sem[f"st{par}"], 16 * ((t_ - 2) // 2 + 1))
            inc(sync.dma_start(out=SCT[:, par, :], in_=s_t[:, t_ * 128:(t_ + 1) * 128],
                               transpose=True), f"sc{par}", 16)
            inc(sync.dma_start(out=XTI[:, par, :], in_=x_my[t_ * 128:(t_ + 1) * 128, :]),
                f"sc{par}", 16)

    @block.vector
    def _(v):
        if combine_mode == 1:
            return
        for t_ in range(NT):
            par = t_ % 2
            v.wait_ge(sem[f"ga{par}"], 32 * (t_ // 2 + 1))
            if t_ >= 2 and combine_mode == 0:
                v.wait_ge(sem[f"st{par}"], 16 * ((t_ - 2) // 2 + 1))
            v.tensor_scalar(CW1[:, par, :], AG0[:, par, :], GATE0[:, t_:t_ + 1], None, ALU.mult)
            v.drain()
            inc(v.scalar_tensor_tensor(CW2[:, par, :], AG1[:, par, :], GATE1[:, t_:t_ + 1],
                                       CW1[:, par, :], ALU.mult, ALU.add), "evv", 1)
            v.wait_ge(sem["eva"], evaC0 + t_ + 1)
            v.wait_ge(sem[f"sc{par}"], 32 * (t_ // 2 + 1))
            v.tensor_tensor(CW2[:, par, :], CW1[:, par, :], SCT[:, par, :], ALU.mult)
            v.drain()
            inc(v.tensor_tensor(OUTT[:, par, :], CW2[:, par, :], XTI[:, par, :],
                                ALU.add), "evv", 1)

    @block.scalar
    def _(s):
        if combine_mode == 1:
            return
        for t_ in range(NT):
            par = t_ % 2
            s.wait_ge(sem["evv"], evvC0 + 2 * t_ + 1)
            inc(s.activation(CW1[:, par, :], CW2[:, par, :], AF.Silu), "eva", 1)

    if debug:
        snF = snap()

        @block.gpsimd
        def _(g):
            join(g, snF)
            inc(g.dma_start(out=dbg["xn"], in_=xn_bf), "sto", 16)
            inc(g.dma_start(out=dbg["trs"], in_=t_rs), "sto", 16)
            inc(g.dma_start(out=dbg["dmy"], in_=d_my), "sto", 16)
            inc(g.dma_start(out=dbg["ymy"], in_=y_my), "sto", 16)
            inc(g.dma_start(out=dbg["st"], in_=s_t), "sto", 16)
            inc(g.dma_start(out=dbg["slot0"], in_=TSLOT0[:]), "sto", 16)
            inc(g.dma_start(out=dbg["slot1"], in_=TSLOT1[:]), "sto", 16)
            inc(g.dma_start(out=dbg["gate0"], in_=GATE0[:]), "sto", 16)
            inc(g.dma_start(out=dbg["gate1"], in_=GATE1[:]), "sto", 16)
            inc(g.dma_start(out=dbg["yidx0"], in_=YIDX0[:]), "sto", 16)
            inc(g.dma_start(out=dbg["yidx1"], in_=YIDX1[:]), "sto", 16)

    @block.sync
    def _(sync):
        for nm in ("st0", "st1", "sto", "ga0", "ga1", "sc0", "sc1"):
            if cnt[nm] > 0:
                sync.wait_ge(sem[nm], cnt[nm])

    arenaC.close()
    stack.close()
    return nc


def _in_maps(inputs):
    x = np.ascontiguousarray(inputs["L0_attn_x"], dtype=np.float32)
    x2d = x.reshape(65536, D)
    x_aug = np.zeros((65537, D), np.float32)
    x_aug[:65536] = x2d

    def f32c(a):
        return np.ascontiguousarray(a, dtype=np.float32)

    shared = {
        "wx_f": f32c(inputs["L0_attn_fwd_Wx"]), "wgx_f": f32c(inputs["L0_attn_fwd_Wgx"]),
        "wga_f": f32c(inputs["L0_attn_fwd_Wga"]), "wo_f": f32c(inputs["L0_attn_fwd_Wo"]),
        "wx_b": f32c(inputs["L0_attn_bwd_Wx"]), "wgx_b": f32c(inputs["L0_attn_bwd_Wgx"]),
        "wga_b": f32c(inputs["L0_attn_bwd_Wga"]), "wo_b": f32c(inputs["L0_attn_bwd_Wo"]),
        "ld_w": f32c(inputs["L0_attn_ld_W"]),
        "bx_f": f32c(inputs["L0_attn_fwd_bx"]), "bgx_f": f32c(inputs["L0_attn_fwd_bgx"]),
        "bga_f": f32c(inputs["L0_attn_fwd_bga"]), "bo_f": f32c(inputs["L0_attn_fwd_bo"]),
        "bx_b": f32c(inputs["L0_attn_bwd_bx"]), "bgx_b": f32c(inputs["L0_attn_bwd_bgx"]),
        "bga_b": f32c(inputs["L0_attn_bwd_bga"]), "bo_b": f32c(inputs["L0_attn_bwd_bo"]),
        "ld_b": f32c(inputs["L0_attn_ld_b"]),
        "alog_f": f32c(inputs["L0_attn_fwd_a_logit"]),
        "alog_b": f32c(inputs["L0_attn_bwd_a_logit"]),
        "wg": f32c(inputs["L0_ffn_wg"]),
        "c_u128": np.triu(np.ones((128, 128), np.float32)),
        "c_id": np.eye(128, dtype=np.float32),
        "c_onesrow": np.ones((1, 128), np.float32),
    }
    w1 = f32c(inputs["L0_ffn_w1"])
    b1 = f32c(inputs["L0_ffn_b1"])
    w2 = f32c(inputs["L0_ffn_w2"])
    b2 = f32c(inputs["L0_ffn_b2"])

    maps = []
    for c in range(NCORES):
        m = dict(shared)
        m["x_my"] = np.ascontiguousarray(x2d[c * TOK:(c + 1) * TOK])
        m["w1_my"] = np.ascontiguousarray(w1[c])
        m["b1_my"] = np.ascontiguousarray(b1[c])
        m["w2_my"] = np.ascontiguousarray(w2[c])
        m["b2_my"] = np.ascontiguousarray(b2[c])
        mask = np.zeros((8, 2), np.float32)
        mask[:c, 0] = 1.0
        mask[:, 1] = 1.0
        m["c_mask2"] = mask
        tokp = np.zeros((128, NT, 2), np.float32)
        tid = c * TOK + np.arange(NT)[None, :] * 128 + np.arange(128)[:, None]
        s_of = (tid % TOK) // SEQ
        r_of = tid % SEQ
        row = s_of * (8 * SEQ) + c * SEQ + r_of
        tokp[:, :, 0] = row + 1
        tokp[:, :, 1] = row + 1
        m["c_tokp"] = np.ascontiguousarray(tokp.reshape(128, NT * 2))
        maps.append(m)
    return maps


_NC_CACHE = {}


def kernel(**inputs):
    maps = _in_maps(inputs)
    if "prog" not in _NC_CACHE:
        _NC_CACHE["prog"] = build_program(debug=False)
    nc = _NC_CACHE["prog"]
    res = run_bass_kernel_spmd(nc, maps, core_ids=list(range(NCORES)))
    out = np.concatenate([np.asarray(res.results[c]["out_my"], dtype=np.float32)
                          for c in range(NCORES)], axis=0)
    return out.reshape(32, SEQ, D)
